# version 11
# baseline (speedup 1.0000x reference)
import numpy as np
import ml_dtypes
from contextlib import ExitStack

import concourse.bass as bass
import concourse.mybir as mybir
from concourse.bass_utils import run_bass_kernel_spmd

F32 = mybir.dt.float32
BF16 = mybir.dt.bfloat16
AF = mybir.ActivationFunctionType
ALU = mybir.AluOpType
AX = mybir.AxisListType

T = 1024
D = 4096
S = 8192
NH = 32
NHG = 4
HPG = 8
NKB = 16
KBS = 512
NES = 8
ES = 512
NTT = 8
NDC = 32
WSLOTS = 24
WSH = 512
EXP_SCALE = 1.0 / float(np.sqrt(128.0))
EXP_BIAS = -15.0
EPS = 1e-5
RG = [list(range(8))]


def build_program(debug=False, phases=3):
    nc = bass.Bass("TRN2", target_bir_lowering=False)

    x_in = nc.dram_tensor("x", [T, D], F32, kind="ExternalInput")
    wsh_in = {w: nc.dram_tensor(f"w{w}_sh", [WSH, D], F32, kind="ExternalInput")
              for w in ("q", "k", "v", "o", "f")}
    ones_f128_in = nc.dram_tensor("ones_f128", [128, 128], F32, kind="ExternalInput")
    bias_in = nc.dram_tensor("bias", [1, D], F32, kind="ExternalInput")
    consts_in = nc.dram_tensor("consts", [128, 8], F32, kind="ExternalInput")
    ones_bf_in = nc.dram_tensor("ones_bf", [128, 128], BF16, kind="ExternalInput")
    ones_f_in = nc.dram_tensor("ones_f", [1, 128], F32, kind="ExternalInput")
    out_t = nc.dram_tensor("out", [T, D], F32, kind="ExternalOutput")
    dbg = {}
    if debug:
        dbg["xn"] = nc.dram_tensor("dbg_xn", [T, D], BF16, kind="ExternalOutput")
        dbg["q"] = nc.dram_tensor("dbg_q", [T, D], BF16, kind="ExternalOutput")
        dbg["g"] = nc.dram_tensor("dbg_g", [T, D], F32, kind="ExternalOutput")
        dbg["o2"] = nc.dram_tensor("dbg_o2", [D, T], BF16, kind="ExternalOutput")
        dbg["wq"] = nc.dram_tensor("dbg_wq", [D, D], BF16, kind="ExternalOutput")
        dbg["wf"] = nc.dram_tensor("dbg_wf", [D, D], BF16, kind="ExternalOutput")
        dbg["kg"] = nc.dram_tensor("dbg_kg", [S, D], BF16, kind="ExternalOutput")
        dbg["vg"] = nc.dram_tensor("dbg_vg", [S, D], BF16, kind="ExternalOutput")

    wsh_bf = {w: nc.dram_tensor(f"w{w}_shbf", [WSH, D], BF16, kind="Internal")
              for w in ("q", "k", "v", "o", "f")}
    w_bf = {w: nc.dram_tensor(f"w{w}_bf", [D, D], BF16, kind="Internal",
                              addr_space="Shared")
            for w in ("q", "k", "v", "o", "f")}
    xn_dram = nc.dram_tensor("xn_dram", [T, D], BF16, kind="Internal")
    q_dram = nc.dram_tensor("q_dram", [T, D], BF16, kind="Internal")
    k_cin = nc.dram_tensor("k_cin", [T, D], BF16, kind="Internal")
    v_cin = nc.dram_tensor("v_cin", [T, D], BF16, kind="Internal")
    kg = nc.dram_tensor("kg", [S, D], BF16, kind="Internal", addr_space="Shared")
    vg = nc.dram_tensor("vg", [S, D], BF16, kind="Internal", addr_space="Shared")
    g_dram = nc.dram_tensor("g_dram", [T, D], F32, kind="Internal")
    o2_dram = nc.dram_tensor("o2_dram", [D, T], BF16, kind="Internal")

    es_g = ExitStack()
    sem = lambda n: es_g.enter_context(nc.semaphore(n))
    sbufG = lambda n, s, d: es_g.enter_context(nc.sbuf_tensor(n, s, d))

    with es_g:
        ones_bf = sbufG("ones_bf_sb", [128, 128], BF16)
        ones_f = sbufG("ones_f_sb", [1, 128], F32)
        ones_f128 = sbufG("ones_f128_sb", [128, 128], F32)
        c_sb = sbufG("c_sb", [128, 8], F32)
        st = sbufG("st_sb", [128, 16], F32)
        C_EXPS = c_sb[:, 0:1]
        C_EXPB = c_sb[:, 1:2]
        C_I4096 = c_sb[:, 2:3]
        C_EPS = c_sb[:, 3:4]
        C_NHALF = c_sb[:, 4:5]
        C_ONE = c_sb[:, 5:6]
        C_ZERO = c_sb[:, 6:7]

        s_c = sem("s_c")
        s_x0sl = [sem(f"s_x0sl{i}") for i in range(2)]
        s_ln = sem("s_ln")
        s_act0 = sem("s_act0")
        s_xnst = sem("s_xnst")
        s_xnT = sem("s_xnT")
        s_cast = sem("s_cast")
        s_wfldsl = [sem(f"s_wfldsl{i}") for i in range(2)]
        s_wfstsl = [sem(f"s_wfstsl{i}") for i in range(2)]
        s_wfsg = sem("s_wfsg")
        s_ag = {w: sem(f"s_ag_{w}") for w in ("q", "k", "v", "f", "o")}
        s_wtsl = [sem(f"s_wtsl{i}") for i in range(WSLOTS)]
        s_slabrel = sem("s_slabrel")
        s_browsl = [sem(f"s_browsl{i}") for i in range(2)]
        s_biasb = sem("s_biasb")
        s_mm1 = sem("s_mm1")
        s_ev1 = sem("s_ev1")
        s_silu = sem("s_silu")
        s_stgsl = [sem(f"s_stgsl{i}") for i in range(4)]
        s_gsl = [sem(f"s_gsl{i}") for i in range(2)]
        s_ccK = sem("s_ccK")
        s_ccV = sem("s_ccV")
        s_qt = sem("s_qt")
        s_kt = sem("s_kt")
        s_v = sem("s_v")
        s_sc = sem("s_sc")
        s_exp = sem("s_exp")
        s_avz = sem("s_avz")
        s_esA = sem("s_esA")
        s_dve_es = sem("s_dve_es")
        s_zmm = sem("s_zmm")
        s_evh = sem("s_evh")
        s_o2 = sem("s_o2")
        s_o2sl = [sem(f"s_o2sl{i}") for i in range(2)]
        s_o2ld = sem("s_o2ld")
        s_wotsl = s_wtsl
        WT_BASE = [1024 // WSLOTS + (1 if i < 1024 % WSLOTS else 0)
                   for i in range(WSLOTS)]
        s_wotrel = sem("s_wotrel")
        s_x3sl = [sem(f"s_x3sl{i}") for i in range(4)]
        s_g3sl = [sem(f"s_g3sl{i}") for i in range(4)]
        s_mm3 = sem("s_mm3")
        s_ev3 = sem("s_ev3")
        s_outsl = [sem(f"s_outsl{i}") for i in range(4)]

        W_ORDER = [("k", k_cin), ("v", v_cin), ("q", q_dram), ("f", None)]

        es_a = ExitStack()
        with es_a:
            sbufA = lambda n, s, d: es_a.enter_context(nc.sbuf_tensor(n, s, d))
            xnT = sbufA("xnT", [128, NDC * T], BF16)
            wt_sb = sbufA("wt_sb", [128, WSLOTS * ES], BF16)
            x_sb = sbufA("x_sb", [128, 2 * D], F32)
            xn_sb = sbufA("xn_sb", [128, D], BF16)
            stage = sbufA("stage", [128, 4 * ES], BF16)
            fstage = sbufA("fstage", [128, 2 * ES], F32)
            gstage = sbufA("gstage", [128, 2 * ES], F32)
            wfb = sbufA("wfb", [128, 2 * 1024], F32)
            wfs = sbufA("wfs", [128, 2 * 1024], BF16)
            brow = sbufA("brow", [1, 2 * ES], F32)
            biasb = sbufA("biasb", [128, D], F32)
            psA = [es_a.enter_context(nc.psum_tensor(f"psA{i}", [128, ES], F32))
                   for i in range(8)]

            blkA = es_a.enter_context(nc.Block("ph01"))

            def mm1_at_group_end(idxp):
                if idxp < 24:
                    return idxp * 8 + 8
                return 200 + (idxp - 24) * 8 + 8

            def emit_slab(eng, wi, wname, esx, dcx):
                G = (wi * NES + esx) * NDC + dcx
                if G >= WSLOTS:
                    idxp, dp = divmod(G - WSLOTS, NDC)
                    if dp == NDC - 1:
                        eng.wait_ge(s_mm1, mm1_at_group_end(idxp))
                    else:
                        eng.wait_ge(s_slabrel, idxp * (NDC - 1) + dp + 1)
                eng.wait_ge(s_ag[wname], 1)
                slot = G % WSLOTS
                eng.dma_start(
                    out=wt_sb[:, slot * ES:slot * ES + ES],
                    in_=w_bf[wname][dcx * 128:dcx * 128 + 128,
                                    esx * ES:esx * ES + ES],
                    ).then_inc(s_wtsl[slot], 16)

            @blkA.gpsimd
            def _(g):
                ncast = 0
                for w in ("k", "v", "q"):
                    g.dma_start(out=wsh_bf[w][:, :], in_=wsh_in[w][:, :]
                                ).then_inc(s_cast, 16)
                    ncast += 1
                    g.wait_ge(s_cast, 16 * ncast)
                    g.collective_compute(
                        "AllGather", ALU.bypass, replica_groups=RG,
                        ins=[wsh_bf[w].ap().opt()], outs=[w_bf[w].ap().opt()],
                    ).then_inc(s_ag[w], 1)
                for j in range(18):
                    if j < 16:
                        rb, chh = j // 4, j % 4
                        if j >= 2:
                            g.wait_ge(s_wfsg, j - 1)
                        g.dma_start(
                            out=wfb[:, (j % 2) * 1024:(j % 2) * 1024 + 1024],
                            in_=wsh_in["f"][rb * 128:rb * 128 + 128,
                                            chh * 1024:chh * 1024 + 1024],
                        ).then_inc(s_wfldsl[j % 2], 16)
                    if j >= 2:
                        js = j - 2
                        rb, chh = js // 4, js % 4
                        g.wait_ge(s_wfsg, js + 1)
                        g.dma_start(
                            out=wsh_bf["f"][rb * 128:rb * 128 + 128,
                                            chh * 1024:chh * 1024 + 1024],
                            in_=wfs[:, (js % 2) * 1024:(js % 2) * 1024 + 1024],
                        ).then_inc(s_wfstsl[js % 2], 16)
                g.wait_ge(s_wfstsl[0], 16 * 8)
                g.wait_ge(s_wfstsl[1], 16 * 8)
                g.collective_compute(
                    "AllGather", ALU.bypass, replica_groups=RG,
                    ins=[wsh_bf["f"].ap().opt()], outs=[w_bf["f"].ap().opt()],
                ).then_inc(s_ag["f"], 1)
                g.dma_start(out=wsh_bf["o"][:, :], in_=wsh_in["o"][:, :]
                            ).then_inc(s_cast, 16)
                n1 = 0
                for wi, (wname, dest) in enumerate(W_ORDER):
                    for esx in range(NES):
                        for tt in range(NTT):
                            if wname == "f":
                                nf = esx * NTT + tt
                                g.wait_ge(s_silu, nf + 1)
                                g.dma_start(
                                    out=g_dram[tt * 128:tt * 128 + 128,
                                               esx * ES:esx * ES + ES],
                                    in_=gstage[:, (nf % 2) * ES:(nf % 2) * ES + ES],
                                ).then_inc(s_gsl[nf % 2], 16)
                            else:
                                g.wait_ge(s_ev1, n1 + 1)
                                g.dma_start(
                                    out=dest[tt * 128:tt * 128 + 128,
                                             esx * ES:esx * ES + ES],
                                    in_=stage[:, (n1 % 4) * ES:(n1 % 4) * ES + ES],
                                ).then_inc(s_stgsl[n1 % 4], 16)
                                n1 += 1
                    if wname == "k":
                        for j in range(4):
                            g.wait_ge(s_stgsl[j], 16 * 16)
                        g.collective_compute(
                            "AllGather", ALU.bypass, replica_groups=RG,
                            ins=[k_cin.ap().opt()], outs=[kg.ap().opt()],
                        ).then_inc(s_ccK, 1)
                    elif wname == "v":
                        for j in range(4):
                            g.wait_ge(s_stgsl[j], 16 * 32)
                        g.collective_compute(
                            "AllGather", ALU.bypass, replica_groups=RG,
                            ins=[v_cin.ap().opt()], outs=[vg.ap().opt()],
                        ).then_inc(s_ccV, 1)
                        g.wait_ge(s_cast, 16 * 4)
                        g.collective_compute(
                            "AllGather", ALU.bypass, replica_groups=RG,
                            ins=[wsh_bf["o"].ap().opt()], outs=[w_bf["o"].ap().opt()],
                        ).then_inc(s_ag["o"], 1)

            @blkA.sync
            def _(s):
                s.dma_start(out=ones_bf[:, :], in_=ones_bf_in[:, :]).then_inc(s_c, 16)
                s.dma_start(out=ones_f[:, :], in_=ones_f_in[:, :]).then_inc(s_c, 16)
                s.dma_start(out=ones_f128[:, :], in_=ones_f128_in[:, :]).then_inc(s_c, 16)
                s.dma_start(out=c_sb[:, :], in_=consts_in[:, :]).then_inc(s_c, 16)
                s.wait_ge(s_xnst, 16 * NTT)
                for dcx in range(NDC):
                    s.dma_start(out=xnT[:, dcx * T:dcx * T + T],
                                in_=xn_dram[0:T, dcx * 128:dcx * 128 + 128],
                                transpose=True).then_inc(s_xnT, 16)
                for wi, (wname, dest) in enumerate(W_ORDER):
                    for esx in range(NES):
                        for dcx in range(NDC):
                            emit_slab(s, wi, wname, esx, dcx)

            @blkA.scalar
            def _(sc):
                sc.wait_ge(s_c, 64)
                for tt in range(2):
                    sc.dma_start(out=x_sb[:, (tt % 2) * D:(tt % 2) * D + D],
                                 in_=x_in[tt * 128:tt * 128 + 128, :]
                                 ).then_inc(s_x0sl[tt % 2], 16)
                for tt in range(NTT):
                    sc.wait_ge(s_ln, 4 * tt + 3)
                    sc.activation(st[:, 3:4], st[:, 2:3], AF.Ln,
                                  bias=C_EPS, scale=C_I4096).then_inc(s_act0, 1)
                    sc.wait_ge(s_act0, 2 * tt + 1)
                    sc.activation(st[:, 4:5], st[:, 3:4], AF.Exp,
                                  bias=C_ZERO, scale=C_NHALF).then_inc(s_act0, 1)
                    sc.wait_ge(s_ln, 4 * tt + 4)
                    sc.dma_start(out=xn_dram[tt * 128:tt * 128 + 128, :],
                                 in_=xn_sb[:, :]).then_inc(s_xnst, 16)
                    if tt + 2 < NTT:
                        sc.dma_start(
                            out=x_sb[:, (tt % 2) * D:(tt % 2) * D + D],
                            in_=x_in[(tt + 2) * 128:(tt + 2) * 128 + 128, :],
                        ).then_inc(s_x0sl[tt % 2], 16)
                for j in range(16):
                    sc.wait_ge(s_wfldsl[j % 2], 16 * (j // 2 + 1))
                    if j >= 2:
                        sc.wait_ge(s_wfstsl[j % 2], 16 * (j // 2))
                    sc.activation(wfs[:, (j % 2) * 1024:(j % 2) * 1024 + 1024],
                                  wfb[:, (j % 2) * 1024:(j % 2) * 1024 + 1024],
                                  AF.Sign, bias=C_ZERO, scale=C_ONE
                                  ).then_inc(s_wfsg, 1)
                for ch in range(8):
                    if ch >= 2:
                        sc.wait_ge(s_biasb, ch - 1)
                    sc.dma_start(out=brow[:, (ch % 2) * ES:(ch % 2) * ES + ES],
                                 in_=bias_in[:, ch * ES:ch * ES + ES]
                                 ).then_inc(s_browsl[ch % 2], 16)
                for esx in range(NES):
                    for ttp in range(NTT):
                        nf = esx * NTT + ttp
                        sc.wait_ge(s_ev1, 192 + nf + 1)
                        if nf >= 2:
                            sc.wait_ge(s_gsl[nf % 2], 16 * (nf // 2))
                        sc.activation(
                            gstage[:, (nf % 2) * ES:(nf % 2) * ES + ES],
                            fstage[:, (nf % 2) * ES:(nf % 2) * ES + ES],
                            AF.Silu, bias=C_ZERO, scale=C_ONE
                            ).then_inc(s_silu, 1)

            @blkA.vector
            def _(v):
                for tt in range(NTT):
                    v.wait_ge(s_x0sl[tt % 2], 16 * (tt // 2 + 1))
                    if tt > 0:
                        v.wait_ge(s_ln, 4 * tt)
                    xt = x_sb[:, (tt % 2) * D:(tt % 2) * D + D]
                    v.reduce_sum(st[:, 0:1], xt, axis=AX.X).then_inc(s_ln, 1)
                    v.wait_ge(s_ln, 4 * tt + 1)
                    v.tensor_scalar_mul(st[:, 1:2], st[:, 0:1], 1.0 / D).then_inc(s_ln, 1)
                    v.wait_ge(s_ln, 4 * tt + 2)
                    if tt > 0:
                        v.wait_ge(s_xnst, 16 * tt)
                    v.scalar_tensor_tensor(
                        xn_sb[:, :], xt, st[:, 1:2], xt,
                        ALU.subtract, ALU.mult, accum_out=st[:, 2:3],
                    ).then_inc(s_ln, 1)
                    v.wait_ge(s_act0, 2 * (tt + 1))
                    v.tensor_scalar(
                        xn_sb[:, :], xt, st[:, 1:2], st[:, 4:5],
                        ALU.subtract, ALU.mult,
                    ).then_inc(s_ln, 1)
                n1 = 0
                for wi, (wname, dest) in enumerate(W_ORDER[:3]):
                    for esx in range(NES):
                        idx = wi * NES + esx
                        for tt in range(NTT):
                            v.wait_ge(s_mm1, idx * NTT + tt + 1)
                            if n1 >= 4:
                                v.wait_ge(s_stgsl[n1 % 4], 16 * (n1 // 4))
                            v.tensor_copy(
                                stage[:, (n1 % 4) * ES:(n1 % 4) * ES + ES],
                                psA[tt][:, :],
                            ).then_inc(s_ev1, 1)
                            n1 += 1
                for ch in range(8):
                    v.wait_ge(s_mm1, 192 + ch + 1)
                    v.tensor_copy(biasb[:, ch * ES:ch * ES + ES], psA[ch][:, :]
                                  ).then_inc(s_biasb, 1)
                nf = 0
                for esx in range(NES):
                    for tt in range(NTT):
                        v.wait_ge(s_mm1, 200 + esx * NTT + tt + 1)
                        if nf == 0:
                            v.wait_ge(s_biasb, 8)
                        if nf >= 2:
                            v.wait_ge(s_silu, nf - 1)
                        v.tensor_add(
                            fstage[:, (nf % 2) * ES:(nf % 2) * ES + ES],
                            psA[tt][:, :],
                            biasb[:, esx * ES:esx * ES + ES],
                        ).then_inc(s_ev1, 1)
                        nf += 1

            @blkA.tensor
            def _(t):
                t.wait_ge(s_c, 64)
                t.wait_ge(s_xnT, 16 * NDC)

                def proj(idx, first_bank_wait):
                    for dcx in range(NDC):
                        G = idx * NDC + dcx
                        slot = G % WSLOTS
                        t.wait_ge(s_wtsl[slot], 16 * (G // WSLOTS + 1))
                        for tt in range(NTT):
                            if dcx == 0:
                                first_bank_wait(tt)
                            mm = t.matmul(
                                psA[tt][:, :],
                                xnT[:, dcx * T + tt * 128:dcx * T + tt * 128 + 128],
                                wt_sb[:, slot * ES:slot * ES + ES],
                                start=(dcx == 0), stop=(dcx == NDC - 1))
                            if dcx == NDC - 1:
                                mm.then_inc(s_mm1, 1)
                            elif tt == NTT - 1:
                                mm.then_inc(s_slabrel, 1)

                for wi, (wname, dest) in enumerate(W_ORDER[:3]):
                    for esx in range(NES):
                        idx = wi * NES + esx
                        if idx == 0:
                            proj(idx, lambda tt: None)
                        else:
                            proj(idx, lambda tt, i=idx: t.wait_ge(
                                s_ev1, (i - 1) * NTT + tt + 1))
                for ch in range(8):
                    t.wait_ge(s_browsl[ch % 2], 16 * (ch // 2 + 1))
                    t.wait_ge(s_ev1, 184 + ch + 1)
                    t.matmul(psA[ch][:, :], ones_f[:, :],
                             brow[:, (ch % 2) * ES:(ch % 2) * ES + ES],
                             start=True, stop=True).then_inc(s_mm1, 1)
                for esx in range(NES):
                    idx = 24 + esx
                    if esx == 0:
                        proj(idx, lambda tt: t.wait_ge(s_biasb, tt + 1))
                    else:
                        proj(idx, lambda tt, e=esx: t.wait_ge(
                            s_ev1, 192 + (e - 1) * NTT + tt + 1))

        if phases < 2:
            if debug:
                with nc.Block("dump1") as blkD:
                    @blkD.gpsimd
                    def _(g):
                        g.wait_ge(s_gsl[0], 16 * 32)
                        g.wait_ge(s_gsl[1], 16 * 32)
                        for w in ("k", "v", "q", "f", "o"):
                            g.wait_ge(s_ag[w], 1)
                        g.wait_ge(s_ccK, 1)
                        g.wait_ge(s_ccV, 1)
                        nd = 0
                        for name, srcb in (("xn", xn_dram), ("q", q_dram),
                                           ("g", g_dram), ("wq", w_bf["q"]),
                                           ("wf", w_bf["f"]), ("kg", kg), ("vg", vg)):
                            g.dma_start(out=dbg[name][:, :], in_=srcb[:, :]
                                        ).then_inc(s_o2ld, 16)
                            nd += 1
                        g.wait_ge(s_o2ld, 16 * nd)
            return nc
        es_b = ExitStack()
        with es_b:
            sbufB = lambda n, s, d: es_b.enter_context(nc.sbuf_tensor(n, s, d))
            NB = NH * NKB
            KSL = NKB * KBS
            VSL = NKB * 4 * 256
            qt_sb = sbufB("qt_sb", [128, 3 * T], BF16)
            kt_sb = sbufB("kt_sb", [128, 2 * KSL], BF16)
            v_sb = sbufB("v_sb", [128, 2 * VSL], BF16)
            e_sb = sbufB("e_sb", [128, 8 * T], BF16)
            eA_sb = sbufB("eA_sb", [128, 2 * T], BF16)
            eB_sb = sbufB("eB_sb", [128, 2 * T], BF16)
            esum_sb = sbufB("esum_sb", [128, 2 * T], BF16)
            av_buf = sbufB("av_buf", [128, 2 * T], F32)
            z_buf = sbufB("z_buf", [128, 2 * T], F32)
            zinv = sbufB("zinv", [128, T], F32)
            o2st_sb = sbufB("o2st_sb", [128, 2 * T], BF16)
            ps_sc = [es_b.enter_context(nc.psum_tensor(f"ps_sc{i}", [128, T], F32))
                     for i in range(2)]
            ps_av = es_b.enter_context(nc.psum_tensor("ps_av", [128, T], F32))
            ps_zb = es_b.enter_context(nc.psum_tensor("ps_zb", [128, T], F32))

            blkB = es_b.enter_context(nc.Block("ph2"))

            def ph2_entry(eng):
                for j in range(2):
                    eng.wait_ge(s_gsl[j], 16 * 32)
                eng.wait_ge(s_silu, 64)
                eng.wait_ge(s_ev1, 256)
                eng.wait_ge(s_mm1, 264)

            @blkB.sync
            def _(s):
                s.wait_ge(s_ccK, 1)
                ph2_entry(s)
                for j in range(4):
                    s.wait_ge(s_stgsl[j], 16 * 48)

                def emit_qt(h):
                    s.dma_start(out=qt_sb[:, (h % 3) * T:(h % 3) * T + T],
                                in_=q_dram[0:T, h * 128:h * 128 + 128],
                                transpose=True).then_inc(s_qt, 16)

                def emit_kt(h, kb):
                    kof = (h % 2) * KSL + kb * KBS
                    s.dma_start(out=kt_sb[:, kof:kof + KBS],
                                in_=kg[kb * KBS:kb * KBS + KBS,
                                       h * 128:h * 128 + 128],
                                transpose=True).then_inc(s_kt, 16)

                for h in range(3):
                    emit_qt(h)
                for h in range(2):
                    for kb in range(NKB):
                        emit_kt(h, kb)
                for h in range(NH):
                    if h + 2 < NH:
                        for kb in range(NKB):
                            s.wait_ge(s_avz, 4 * (NKB * h + kb + 1))
                            emit_kt(h + 2, kb)
                    if h + 3 < NH:
                        s.wait_ge(s_avz, 4 * NKB * (h + 1))
                        emit_qt(h + 3)
                    if h >= 1:
                        s.wait_ge(s_o2, 2 * h)
                        s.dma_start(
                            out=o2_dram[(h - 1) * 128:(h - 1) * 128 + 128, :],
                            in_=o2st_sb[:, ((h - 1) % 2) * T:((h - 1) % 2) * T + T],
                        ).then_inc(s_o2sl[(h - 1) % 2], 16)
                s.wait_ge(s_o2, 64)
                s.dma_start(
                    out=o2_dram[31 * 128:31 * 128 + 128, :],
                    in_=o2st_sb[:, T:2 * T],
                ).then_inc(s_o2sl[1], 16)

            @blkB.scalar
            def _(sc):
                ph2_entry(sc)
                for gidx in range(4 * NB):
                    sc.wait_ge(s_sc, gidx + 1)
                    if gidx >= 8:
                        sc.wait_ge(s_avz, gidx - 7)
                        bp, ksp = divmod(gidx - 8, 4)
                        if ksp < 2:
                            sc.wait_ge(s_esA, bp + 1)
                        else:
                            sc.wait_ge(s_dve_es, 2 * bp + 1)
                    sl = (gidx % 8) * T
                    sc.activation(
                        e_sb[:, sl:sl + T],
                        ps_sc[gidx % 2][:, :],
                        AF.Exp, bias=C_EXPB, scale=C_EXPS,
                    ).then_inc(s_exp, 1)

            @blkB.tensor
            def _(t):
                t.wait_ge(s_ev1, 256)

                def sc_part(b):
                    h, kb = divmod(b, NKB)
                    if kb == 0:
                        t.wait_ge(s_qt, 16 * (h + 1))
                    t.wait_ge(s_kt, 16 * (b + 1))
                    kof = (h % 2) * KSL + kb * KBS
                    qof = (h % 3) * T
                    for ks in range(4):
                        gidx = b * 4 + ks
                        if gidx >= 2:
                            t.wait_ge(s_exp, gidx - 1)
                        t.matmul(ps_sc[gidx % 2][:, 0:512],
                                 kt_sb[:, kof + ks * 128:kof + ks * 128 + 128],
                                 qt_sb[:, qof:qof + 512],
                                 start=True, stop=True)
                        t.matmul(ps_sc[gidx % 2][:, 512:1024],
                                 kt_sb[:, kof + ks * 128:kof + ks * 128 + 128],
                                 qt_sb[:, qof + 512:qof + 1024],
                                 start=True, stop=True).then_inc(s_sc, 1)

                def avz_part(b):
                    h, kb = divmod(b, NKB)
                    p, hh = divmod(h, 2)
                    t.wait_ge(s_v, 16 * (NKB * p + kb + 1))
                    if kb == 0 and h >= 1:
                        t.wait_ge(s_evh, 2 * h)
                    st0 = (kb == 0)
                    for ks in range(4):
                        gidx = b * 4 + ks
                        t.wait_ge(s_exp, gidx + 1)
                        sl = (gidx % 8) * T
                        vof = (p % 2) * VSL + kb * 1024 + ks * 256 + hh * 128
                        vtile = v_sb[:, vof:vof + 128]
                        t.matmul(ps_av[:, 0:512], vtile,
                                 e_sb[:, sl:sl + 512],
                                 start=(st0 and ks == 0),
                                 stop=(kb == NKB - 1 and ks == 3))
                        t.matmul(ps_av[:, 512:1024], vtile,
                                 e_sb[:, sl + 512:sl + 1024],
                                 start=(st0 and ks == 0),
                                 stop=(kb == NKB - 1 and ks == 3)
                                 ).then_inc(s_avz, 1)
                    t.wait_ge(s_dve_es, 2 * (b + 1))
                    eso = (b % 2) * T
                    t.matmul(ps_zb[:, 0:512], ones_bf[:, :],
                             esum_sb[:, eso:eso + 512],
                             start=st0, stop=(kb == NKB - 1))
                    t.matmul(ps_zb[:, 512:1024], ones_bf[:, :],
                             esum_sb[:, eso + 512:eso + 1024],
                             start=st0, stop=(kb == NKB - 1)
                             ).then_inc(s_zmm, 1)

                sc_part(0)
                for b in range(NB):
                    if b + 1 < NB:
                        sc_part(b + 1)
                    avz_part(b)

            @blkB.vector
            def _(v):
                ph2_entry(v)
                for b in range(NB):
                    h, kb = divmod(b, NKB)
                    v.wait_ge(s_exp, 4 * b + 4)
                    if b >= 2:
                        v.wait_ge(s_zmm, b - 1)
                    eo = (b % 2) * T
                    s2 = ((4 * b + 2) % 8) * T
                    s3 = ((4 * b + 3) % 8) * T
                    v.tensor_add(eB_sb[:, eo:eo + T],
                                 e_sb[:, s2:s2 + T], e_sb[:, s3:s3 + T]
                                 ).then_inc(s_dve_es, 1)
                    v.wait_ge(s_dve_es, 2 * b + 1)
                    v.wait_ge(s_esA, b + 1)
                    v.tensor_add(esum_sb[:, eo:eo + T],
                                 eA_sb[:, eo:eo + T], eB_sb[:, eo:eo + T]
                                 ).then_inc(s_dve_es, 1)
                    if kb == NKB - 1:
                        ho = (h % 2) * T
                        if h >= 2:
                            v.wait_ge(s_o2, 2 * (h - 1))
                        v.wait_ge(s_avz, 4 * NKB * (h + 1))
                        v.tensor_copy(av_buf[:, ho:ho + T], ps_av[:, :]
                                      ).then_inc(s_evh, 1)
                        v.wait_ge(s_zmm, NKB * (h + 1))
                        v.tensor_copy(z_buf[:, ho:ho + T], ps_zb[:, :]
                                      ).then_inc(s_evh, 1)
                        v.wait_ge(s_evh, 2 * h + 2)
                        v.wait_ge(s_o2, 2 * h)
                        v.reciprocal(zinv[:, :], z_buf[:, ho:ho + T]
                                     ).then_inc(s_o2, 1)
                        v.wait_ge(s_o2, 2 * h + 1)
                        if h >= 2:
                            v.wait_ge(s_o2sl[h % 2], 16 * (h // 2))
                        v.tensor_mul(o2st_sb[:, ho:ho + T],
                                     av_buf[:, ho:ho + T], zinv[:, :]
                                     ).then_inc(s_o2, 1)

            @blkB.gpsimd
            def _(g):
                g.wait_ge(s_ccV, 1)
                ph2_entry(g)

                def emit_v(p, kb):
                    vof = (p % 2) * VSL + kb * 1024
                    g.dma_start(
                        out=v_sb[:, vof:vof + 1024].rearrange(
                            "p (ks c) -> p ks c", ks=4),
                        in_=vg[kb * KBS:kb * KBS + KBS,
                               p * 256:p * 256 + 256].rearrange(
                            "(ks k) c -> k ks c", k=128),
                    ).then_inc(s_v, 16)

                for p in range(2):
                    for kb in range(NKB):
                        emit_v(p, kb)
                for b in range(NB):
                    h, kb = divmod(b, NKB)
                    if h % 2 == 0 and h >= 2 and h + 2 < NH:
                        g.wait_ge(s_avz, 4 * (NKB * (h - 1) + kb + 1))
                        emit_v(h // 2 + 1, kb)
                    g.wait_ge(s_exp, 4 * b + 2)
                    if b >= 2:
                        g.wait_ge(s_zmm, b - 1)
                    eo = (b % 2) * T
                    s0 = ((4 * b) % 8) * T
                    s1 = ((4 * b + 1) % 8) * T
                    g.tensor_tensor(
                        eA_sb[:, eo:eo + T],
                        e_sb[:, s0:s0 + T], e_sb[:, s1:s1 + T],
                        ALU.add).then_inc(s_esA, 1)

        if phases < 3:
            if debug:
                with nc.Block("dump2") as blkD:
                    @blkD.gpsimd
                    def _(g):
                        g.wait_ge(s_o2sl[0], 16 * 16)
                        g.wait_ge(s_o2sl[1], 16 * 16)
                        g.wait_ge(s_gsl[0], 16 * 32)
                        g.wait_ge(s_gsl[1], 16 * 32)
                        nd = 0
                        for name, srcb in (("xn", xn_dram), ("q", q_dram),
                                           ("g", g_dram), ("o2", o2_dram),
                                           ("wq", w_bf["q"]), ("wf", w_bf["f"]),
                                           ("kg", kg), ("vg", vg)):
                            g.dma_start(out=dbg[name][:, :], in_=srcb[:, :]
                                        ).then_inc(s_o2ld, 16)
                            nd += 1
                        g.wait_ge(s_o2ld, 16 * nd)
            return nc
        es_c = ExitStack()
        with es_c:
            sbufC = lambda n, s, d: es_c.enter_context(nc.sbuf_tensor(n, s, d))
            o2_sb = sbufC("o2_sb", [128, NDC * T], BF16)
            wot_sb = sbufC("wot_sb", [128, WSLOTS * ES], BF16)
            x3_sb = sbufC("x3_sb", [128, 4 * ES], F32)
            g3_sb = sbufC("g3_sb", [128, 4 * ES], F32)
            res_sb = sbufC("res_sb", [128, 4 * ES], F32)
            psB = [es_c.enter_context(nc.psum_tensor(f"psB{i}", [128, ES], F32))
                   for i in range(8)]

            blkC = es_c.enter_context(nc.Block("ph3"))

            def emit_wot(eng, os_, e):
                G = os_ * NDC + e
                if G >= WSLOTS:
                    idxp, dp = divmod(G - WSLOTS, NDC)
                    if dp == NDC - 1:
                        eng.wait_ge(s_mm3, idxp * NTT + NTT)
                    else:
                        eng.wait_ge(s_wotrel, idxp * (NDC - 1) + dp + 1)
                slot = G % WSLOTS
                eng.dma_start(
                    out=wot_sb[:, slot * ES:slot * ES + ES],
                    in_=w_bf["o"][e * 128:e * 128 + 128, os_ * ES:os_ * ES + ES],
                ).then_inc(s_wotsl[slot], 16)

            def ph3_barrier(eng):
                eng.wait_ge(s_avz, 2048)
                eng.wait_ge(s_o2, 64)
                eng.wait_ge(s_o2sl[0], 16 * 16)
                eng.wait_ge(s_o2sl[1], 16 * 16)
                eng.wait_ge(s_ag["o"], 1)

            @blkC.sync
            def _(s):
                ph3_barrier(s)
                for os_ in range(NES):
                    for tt in range(NTT):
                        n = os_ * NTT + tt
                        if n >= 4:
                            s.wait_ge(s_ev3, 2 * (n - 3))
                        s.dma_start(
                            out=x3_sb[:, (n % 4) * ES:(n % 4) * ES + ES],
                            in_=x_in[tt * 128:tt * 128 + 128, os_ * ES:os_ * ES + ES],
                        ).then_inc(s_x3sl[n % 4], 16)

            @blkC.scalar
            def _(sc):
                ph3_barrier(sc)
                for os_ in range(NES):
                    for e in range(NDC):
                        emit_wot(sc, os_, e)

            @blkC.gpsimd
            def _(g):
                g.wait_ge(s_o2sl[0], 16 * 16)
                g.wait_ge(s_o2sl[1], 16 * 16)
                for e in range(NDC):
                    g.dma_start(out=o2_sb[:, e * T:e * T + T],
                                in_=o2_dram[e * 128:e * 128 + 128, :]
                                ).then_inc(s_o2ld, 16)
                g.wait_ge(s_gsl[0], 16 * 32)
                g.wait_ge(s_gsl[1], 16 * 32)
                for n in range(68):
                    if n < 64:
                        os_, tt = n // NTT, n % NTT
                        if n >= 4:
                            g.wait_ge(s_ev3, 2 * (n - 3))
                        g.dma_start(
                            out=g3_sb[:, (n % 4) * ES:(n % 4) * ES + ES],
                            in_=g_dram[tt * 128:tt * 128 + 128,
                                       os_ * ES:os_ * ES + ES],
                        ).then_inc(s_g3sl[n % 4], 16)
                    if n >= 4:
                        ns = n - 4
                        os_, tt = ns // NTT, ns % NTT
                        g.wait_ge(s_ev3, 2 * (ns + 1))
                        g.dma_start(
                            out=out_t[tt * 128:tt * 128 + 128,
                                      os_ * ES:os_ * ES + ES],
                            in_=res_sb[:, (ns % 4) * ES:(ns % 4) * ES + ES],
                        ).then_inc(s_outsl[ns % 4], 16)
                for j in range(4):
                    g.wait_ge(s_outsl[j], 16 * 16)
                if debug:
                    g.dma_start(out=dbg["xn"][:, :], in_=xn_dram[:, :]).then_inc(s_o2ld, 16)
                    g.dma_start(out=dbg["q"][:, :], in_=q_dram[:, :]).then_inc(s_o2ld, 16)
                    g.dma_start(out=dbg["g"][:, :], in_=g_dram[:, :]).then_inc(s_o2ld, 16)
                    g.dma_start(out=dbg["o2"][:, :], in_=o2_dram[:, :]).then_inc(s_o2ld, 16)
                    g.dma_start(out=dbg["wq"][:, :], in_=w_bf["q"][:, :]).then_inc(s_o2ld, 16)
                    g.dma_start(out=dbg["wf"][:, :], in_=w_bf["f"][:, :]).then_inc(s_o2ld, 16)
                    g.dma_start(out=dbg["kg"][:, :], in_=kg[:, :]).then_inc(s_o2ld, 16)
                    g.dma_start(out=dbg["vg"][:, :], in_=vg[:, :]).then_inc(s_o2ld, 16)
                    g.wait_ge(s_o2ld, 16 * (NDC + 8))

            @blkC.tensor
            def _(t):
                t.wait_ge(s_o2, 64)
                t.wait_ge(s_exp, 2048)
                t.wait_ge(s_zmm, 512)
                t.wait_ge(s_o2ld, 16 * NDC)
                for os_ in range(NES):
                    for e in range(NDC):
                        G = os_ * NDC + e
                        slot = G % WSLOTS
                        t.wait_ge(s_wotsl[slot],
                                  16 * (WT_BASE[slot] + G // WSLOTS + 1))
                        for tt in range(NTT):
                            if e == 0 and os_ >= 1:
                                t.wait_ge(s_ev3, 2 * ((os_ - 1) * NTT + tt + 1))
                            mm = t.matmul(
                                psB[tt][:, :],
                                o2_sb[:, e * T + tt * 128:e * T + tt * 128 + 128],
                                wot_sb[:, slot * ES:slot * ES + ES],
                                start=(e == 0), stop=(e == NDC - 1))
                            if e == NDC - 1:
                                mm.then_inc(s_mm3, 1)
                            elif tt == NTT - 1:
                                mm.then_inc(s_wotrel, 1)

            @blkC.vector
            def _(v):
                for os_ in range(NES):
                    for tt in range(NTT):
                        n = os_ * NTT + tt
                        v.wait_ge(s_mm3, n + 1)
                        v.wait_ge(s_g3sl[n % 4], 16 * (n // 4 + 1))
                        v.wait_ge(s_x3sl[n % 4], 16 * (n // 4 + 1))
                        if n >= 4:
                            v.wait_ge(s_outsl[n % 4], 16 * (n // 4))
                        rof = (n % 4) * ES
                        v.tensor_mul(res_sb[:, rof:rof + ES], psB[tt][:, :],
                                     g3_sb[:, rof:rof + ES]).then_inc(s_ev3, 1)
                        v.wait_ge(s_ev3, 2 * n + 1)
                        v.tensor_add(res_sb[:, rof:rof + ES],
                                     res_sb[:, rof:rof + ES],
                                     x3_sb[:, rof:rof + ES]).then_inc(s_ev3, 1)

    return nc


_NC_CACHE = None


def _get_nc():
    global _NC_CACHE
    if _NC_CACHE is None:
        _NC_CACHE = build_program()
    return _NC_CACHE


def kernel(L0_attn_x, L0_attn_w_q, L0_attn_w_k, L0_attn_w_v, L0_attn_w_o,
           L0_ffn_weight, L0_ffn_bias):
    x = np.ascontiguousarray(np.asarray(L0_attn_x, dtype=np.float32).reshape(S, D))
    ws = {
        "q": np.asarray(L0_attn_w_q, dtype=np.float32).T,
        "k": np.asarray(L0_attn_w_k, dtype=np.float32).T,
        "v": np.asarray(L0_attn_w_v, dtype=np.float32).T,
        "o": np.asarray(L0_attn_w_o, dtype=np.float32).T,
        "f": np.asarray(L0_ffn_weight, dtype=np.float32).T,
    }
    bias = np.ascontiguousarray(
        np.asarray(L0_ffn_bias, dtype=np.float32).reshape(1, D))

    consts = np.zeros((128, 8), dtype=np.float32)
    consts[:, 0] = EXP_SCALE
    consts[:, 1] = EXP_BIAS
    consts[:, 2] = 1.0 / 4096.0
    consts[:, 3] = EPS
    consts[:, 4] = -0.5
    consts[:, 5] = 1.0
    consts[:, 6] = 0.0
    ones_bf = np.ones((128, 128), dtype=ml_dtypes.bfloat16)
    ones_f = np.ones((1, 128), dtype=np.float32)

    nc = _get_nc()
    ones_f128 = np.ones((128, 128), dtype=np.float32)
    in_maps = []
    for c in range(8):
        m = {
            "x": x[c * T:(c + 1) * T, :],
            "bias": bias, "consts": consts,
            "ones_bf": ones_bf, "ones_f": ones_f, "ones_f128": ones_f128,
        }
        for w in ("q", "k", "v", "o", "f"):
            m[f"w{w}_sh"] = np.ascontiguousarray(ws[w][c * WSH:(c + 1) * WSH, :])
        in_maps.append(m)
    res = run_bass_kernel_spmd(nc, in_maps, core_ids=list(range(8)))
    out = np.concatenate([np.asarray(r["out"]) for r in res.results], axis=0)
    return out.reshape(1, S, D).astype(np.float32)


# version 18 (final)
# speedup vs baseline: 1.0965x; 1.0965x over previous
import numpy as np
import ml_dtypes
from contextlib import ExitStack

import concourse.bass as bass
import concourse.mybir as mybir
from concourse.bass_utils import run_bass_kernel_spmd

F32 = mybir.dt.float32
BF16 = mybir.dt.bfloat16
AF = mybir.ActivationFunctionType
ALU = mybir.AluOpType
AX = mybir.AxisListType

T = 1024
D = 4096
S = 8192
NH = 32
NHG = 4
HPG = 8
NKB = 16
KBS = 512
NES = 8
ES = 512
NTT = 8
NDC = 32
WSLOTS = 16
WSH = 512
EXP_SCALE = 1.0 / float(np.sqrt(128.0))
EXP_BIAS = -15.0
EPS = 1e-5
RG = [list(range(8))]


def build_program(debug=False, phases=3):
    nc = bass.Bass("TRN2", target_bir_lowering=False)

    x_in = nc.dram_tensor("x", [T, D], F32, kind="ExternalInput")
    wsh_in = {w: nc.dram_tensor(f"w{w}_sh", [WSH, D], F32, kind="ExternalInput")
              for w in ("q", "k", "v", "o", "f")}
    bias_in = nc.dram_tensor("bias", [1, D], F32, kind="ExternalInput")
    consts_in = nc.dram_tensor("consts", [128, 8], F32, kind="ExternalInput")
    ones_bf_in = nc.dram_tensor("ones_bf", [128, 128], BF16, kind="ExternalInput")
    ones_f_in = nc.dram_tensor("ones_f", [1, 128], F32, kind="ExternalInput")
    out_t = nc.dram_tensor("out", [T, D], F32, kind="ExternalOutput")
    dbg = {}
    if debug:
        dbg["xn"] = nc.dram_tensor("dbg_xn", [T, D], BF16, kind="ExternalOutput")
        dbg["q"] = nc.dram_tensor("dbg_q", [T, D], BF16, kind="ExternalOutput")
        dbg["g"] = nc.dram_tensor("dbg_g", [T, D], F32, kind="ExternalOutput")
        dbg["o2"] = nc.dram_tensor("dbg_o2", [D, T], BF16, kind="ExternalOutput")
        dbg["wq"] = nc.dram_tensor("dbg_wq", [D, D], BF16, kind="ExternalOutput")
        dbg["wf"] = nc.dram_tensor("dbg_wf", [D, D], BF16, kind="ExternalOutput")
        dbg["kg"] = nc.dram_tensor("dbg_kg", [S, D], BF16, kind="ExternalOutput")
        dbg["vg"] = nc.dram_tensor("dbg_vg", [S, D], BF16, kind="ExternalOutput")

    wsh_bf = {w: nc.dram_tensor(f"w{w}_shbf", [WSH, D], BF16, kind="Internal")
              for w in ("q", "k", "v", "o", "f")}
    w_bf = {w: nc.dram_tensor(f"w{w}_bf", [D, D], BF16, kind="Internal",
                              addr_space="Shared")
            for w in ("q", "k", "v", "o", "f")}
    xn_dram = nc.dram_tensor("xn_dram", [T, D], BF16, kind="Internal")
    q_dram = nc.dram_tensor("q_dram", [T, D], BF16, kind="Internal")
    k_cin = nc.dram_tensor("k_cin", [T, D], BF16, kind="Internal")
    v_cin = nc.dram_tensor("v_cin", [T, D], BF16, kind="Internal")
    kg = nc.dram_tensor("kg", [S, D], BF16, kind="Internal", addr_space="Shared")
    vg = nc.dram_tensor("vg", [S, D], BF16, kind="Internal", addr_space="Shared")
    g_dram = nc.dram_tensor("g_dram", [T, D], F32, kind="Internal")
    o2_dram = nc.dram_tensor("o2_dram", [D, T], BF16, kind="Internal")

    es_g = ExitStack()
    sem = lambda n: es_g.enter_context(nc.semaphore(n))
    sbufG = lambda n, s, d: es_g.enter_context(nc.sbuf_tensor(n, s, d))

    with es_g:
        ones_bf = sbufG("ones_bf_sb", [128, 128], BF16)
        ones_f = sbufG("ones_f_sb", [1, 128], F32)
        c_sb = sbufG("c_sb", [128, 8], F32)
        st = sbufG("st_sb", [128, 16], F32)
        C_EXPS = c_sb[:, 0:1]
        C_EXPB = c_sb[:, 1:2]
        C_I4096 = c_sb[:, 2:3]
        C_EPS = c_sb[:, 3:4]
        C_NHALF = c_sb[:, 4:5]
        C_ONE = c_sb[:, 5:6]
        C_ZERO = c_sb[:, 6:7]

        s_c = sem("s_c")
        s_x0sl = [sem(f"s_x0sl{i}") for i in range(2)]
        s_ln = sem("s_ln")
        s_act0 = sem("s_act0")
        s_xnst = sem("s_xnst")
        s_xnT = sem("s_xnT")
        s_cast = sem("s_cast")
        s_wfldsl = [sem(f"s_wfldsl{i}") for i in range(2)]
        s_wfstsl = [sem(f"s_wfstsl{i}") for i in range(2)]
        s_wfsg = sem("s_wfsg")
        s_ag = {w: sem(f"s_ag_{w}") for w in ("q", "k", "v", "f", "o")}
        s_wtsl = [sem(f"s_wtsl{i}") for i in range(WSLOTS)]
        s_slabrel = sem("s_slabrel")
        s_browsl = [sem(f"s_browsl{i}") for i in range(2)]
        s_biasb = sem("s_biasb")
        s_mm1 = sem("s_mm1")
        s_ev1 = sem("s_ev1")
        s_silu = sem("s_silu")
        s_stgsl = [sem(f"s_stgsl{i}") for i in range(4)]
        s_gsl = [sem(f"s_gsl{i}") for i in range(2)]
        s_ccK = sem("s_ccK")
        s_ccV = sem("s_ccV")
        s_qt = sem("s_qt")
        s_kt = sem("s_kt")
        s_v = sem("s_v")
        s_sc = sem("s_sc")
        s_exp = sem("s_exp")
        s_avz = sem("s_avz")
        s_esA = sem("s_esA")
        s_dve_es = sem("s_dve_es")
        s_zmm = sem("s_zmm")
        s_evh = sem("s_evh")
        s_o2 = sem("s_o2")
        s_o2sl = [sem(f"s_o2sl{i}") for i in range(2)]
        s_o2ld = sem("s_o2ld")
        s_wotsl = s_wtsl
        WT_BASE = [1024 // WSLOTS + (1 if i < 1024 % WSLOTS else 0)
                   for i in range(WSLOTS)]
        s_wotrel = sem("s_wotrel")
        s_x3sl = [sem(f"s_x3sl{i}") for i in range(4)]
        s_g3sl = [sem(f"s_g3sl{i}") for i in range(4)]
        s_mm3 = sem("s_mm3")
        s_ev3 = sem("s_ev3")
        s_outsl = [sem(f"s_outsl{i}") for i in range(4)]

        W_ORDER = [("k", k_cin), ("v", v_cin), ("q", q_dram), ("f", None)]

        es_a = ExitStack()
        with es_a:
            sbufA = lambda n, s, d: es_a.enter_context(nc.sbuf_tensor(n, s, d))
            xnT = sbufA("xnT", [128, NDC * T], BF16)
            wt_sb = sbufA("wt_sb", [128, WSLOTS * ES], BF16)
            x_sb = sbufA("x_sb", [128, 2 * D], F32)
            xn_sb = sbufA("xn_sb", [128, D], BF16)
            stage = sbufA("stage", [128, 4 * ES], BF16)
            fstage = sbufA("fstage", [128, 2 * ES], F32)
            gstage = sbufA("gstage", [128, 2 * ES], F32)
            wfb = sbufA("wfb", [128, 2 * 512], F32)
            wfs = sbufA("wfs", [128, 2 * 512], BF16)
            brow = sbufA("brow", [1, 2 * ES], F32)
            biasb = sbufA("biasb", [128, D], F32)
            psA = [es_a.enter_context(nc.psum_tensor(f"psA{i}", [128, ES], F32))
                   for i in range(8)]

            blkA = es_a.enter_context(nc.Block("ph01"))

            def mm1_at_group_end(idxp):
                if idxp < 24:
                    return idxp * 8 + 8
                return 200 + (idxp - 24) * 8 + 8

            def emit_slab(eng, wi, wname, esx, dcx):
                G = (wi * NES + esx) * NDC + dcx
                if G >= WSLOTS:
                    idxp, dp = divmod(G - WSLOTS, NDC)
                    if dp == NDC - 1:
                        eng.wait_ge(s_mm1, mm1_at_group_end(idxp))
                    else:
                        eng.wait_ge(s_slabrel, idxp * (NDC - 1) + dp + 1)
                eng.wait_ge(s_ag[wname], 1)
                slot = G % WSLOTS
                eng.dma_start(
                    out=wt_sb[:, slot * ES:slot * ES + ES],
                    in_=w_bf[wname][dcx * 128:dcx * 128 + 128,
                                    esx * ES:esx * ES + ES],
                    ).then_inc(s_wtsl[slot], 16)

            @blkA.gpsimd
            def _(g):
                ncast = 0
                for w in ("k", "v", "q"):
                    g.dma_start(out=wsh_bf[w][:, :], in_=wsh_in[w][:, :]
                                ).then_inc(s_cast, 16)
                    ncast += 1
                    g.wait_ge(s_cast, 16 * ncast)
                    g.collective_compute(
                        "AllGather", ALU.bypass, replica_groups=RG,
                        ins=[wsh_bf[w].ap().opt()], outs=[w_bf[w].ap().opt()],
                    ).then_inc(s_ag[w], 1)
                for j in range(34):
                    if j < 32:
                        rb, chh = j // 8, j % 8
                        if j >= 2:
                            g.wait_ge(s_wfsg, j - 1)
                        g.dma_start(
                            out=wfb[:, (j % 2) * 512:(j % 2) * 512 + 512],
                            in_=wsh_in["f"][rb * 128:rb * 128 + 128,
                                            chh * 512:chh * 512 + 512],
                        ).then_inc(s_wfldsl[j % 2], 16)
                    if j >= 2:
                        js = j - 2
                        rb, chh = js // 8, js % 8
                        g.wait_ge(s_wfsg, js + 1)
                        g.dma_start(
                            out=wsh_bf["f"][rb * 128:rb * 128 + 128,
                                            chh * 512:chh * 512 + 512],
                            in_=wfs[:, (js % 2) * 512:(js % 2) * 512 + 512],
                        ).then_inc(s_wfstsl[js % 2], 16)
                g.wait_ge(s_wfstsl[0], 16 * 16)
                g.wait_ge(s_wfstsl[1], 16 * 16)
                g.collective_compute(
                    "AllGather", ALU.bypass, replica_groups=RG,
                    ins=[wsh_bf["f"].ap().opt()], outs=[w_bf["f"].ap().opt()],
                ).then_inc(s_ag["f"], 1)
                g.dma_start(out=wsh_bf["o"][:, :], in_=wsh_in["o"][:, :]
                            ).then_inc(s_cast, 16)
                n1 = 0
                for wi, (wname, dest) in enumerate(W_ORDER):
                    for esx in range(NES):
                        for tt in range(NTT):
                            if wname == "f":
                                nf = esx * NTT + tt
                                g.wait_ge(s_silu, nf + 1)
                                g.dma_start(
                                    out=g_dram[tt * 128:tt * 128 + 128,
                                               esx * ES:esx * ES + ES],
                                    in_=gstage[:, (nf % 2) * ES:(nf % 2) * ES + ES],
                                ).then_inc(s_gsl[nf % 2], 16)
                            else:
                                g.wait_ge(s_ev1, n1 + 1)
                                g.dma_start(
                                    out=dest[tt * 128:tt * 128 + 128,
                                             esx * ES:esx * ES + ES],
                                    in_=stage[:, (n1 % 4) * ES:(n1 % 4) * ES + ES],
                                ).then_inc(s_stgsl[n1 % 4], 16)
                                n1 += 1
                    if wname == "k":
                        for j in range(4):
                            g.wait_ge(s_stgsl[j], 16 * 16)
                        g.collective_compute(
                            "AllGather", ALU.bypass, replica_groups=RG,
                            ins=[k_cin.ap().opt()], outs=[kg.ap().opt()],
                        ).then_inc(s_ccK, 1)
                    elif wname == "v":
                        for j in range(4):
                            g.wait_ge(s_stgsl[j], 16 * 32)
                        g.collective_compute(
                            "AllGather", ALU.bypass, replica_groups=RG,
                            ins=[v_cin.ap().opt()], outs=[vg.ap().opt()],
                        ).then_inc(s_ccV, 1)
                        g.wait_ge(s_cast, 16 * 4)
                        g.collective_compute(
                            "AllGather", ALU.bypass, replica_groups=RG,
                            ins=[wsh_bf["o"].ap().opt()], outs=[w_bf["o"].ap().opt()],
                        ).then_inc(s_ag["o"], 1)

            @blkA.sync
            def _(s):
                s.dma_start(out=ones_bf[:, :], in_=ones_bf_in[:, :]).then_inc(s_c, 16)
                s.dma_start(out=ones_f[:, :], in_=ones_f_in[:, :]).then_inc(s_c, 16)
                s.dma_start(out=c_sb[:, :], in_=consts_in[:, :]).then_inc(s_c, 16)
                s.wait_ge(s_xnst, 16 * NTT)
                for dcx in range(NDC):
                    s.dma_start(out=xnT[:, dcx * T:dcx * T + T],
                                in_=xn_dram[0:T, dcx * 128:dcx * 128 + 128],
                                transpose=True).then_inc(s_xnT, 16)
                for wi, (wname, dest) in enumerate(W_ORDER):
                    for esx in range(NES):
                        for dcx in range(NDC):
                            emit_slab(s, wi, wname, esx, dcx)

            @blkA.scalar
            def _(sc):
                sc.wait_ge(s_c, 48)
                for tt in range(2):
                    sc.dma_start(out=x_sb[:, (tt % 2) * D:(tt % 2) * D + D],
                                 in_=x_in[tt * 128:tt * 128 + 128, :]
                                 ).then_inc(s_x0sl[tt % 2], 16)
                for tt in range(NTT):
                    sc.wait_ge(s_ln, 4 * tt + 3)
                    sc.activation(st[:, 3:4], st[:, 2:3], AF.Ln,
                                  bias=C_EPS, scale=C_I4096).then_inc(s_act0, 1)
                    sc.wait_ge(s_act0, 2 * tt + 1)
                    sc.activation(st[:, 4:5], st[:, 3:4], AF.Exp,
                                  bias=C_ZERO, scale=C_NHALF).then_inc(s_act0, 1)
                    sc.wait_ge(s_ln, 4 * tt + 4)
                    sc.dma_start(out=xn_dram[tt * 128:tt * 128 + 128, :],
                                 in_=xn_sb[:, :]).then_inc(s_xnst, 16)
                    if tt + 2 < NTT:
                        sc.dma_start(
                            out=x_sb[:, (tt % 2) * D:(tt % 2) * D + D],
                            in_=x_in[(tt + 2) * 128:(tt + 2) * 128 + 128, :],
                        ).then_inc(s_x0sl[tt % 2], 16)
                for j in range(32):
                    sc.wait_ge(s_wfldsl[j % 2], 16 * (j // 2 + 1))
                    if j >= 2:
                        sc.wait_ge(s_wfstsl[j % 2], 16 * (j // 2))
                    sc.activation(wfs[:, (j % 2) * 512:(j % 2) * 512 + 512],
                                  wfb[:, (j % 2) * 512:(j % 2) * 512 + 512],
                                  AF.Sign, bias=C_ZERO, scale=C_ONE
                                  ).then_inc(s_wfsg, 1)
                for ch in range(8):
                    if ch >= 2:
                        sc.wait_ge(s_biasb, ch - 1)
                    sc.dma_start(out=brow[:, (ch % 2) * ES:(ch % 2) * ES + ES],
                                 in_=bias_in[:, ch * ES:ch * ES + ES]
                                 ).then_inc(s_browsl[ch % 2], 16)
                for esx in range(NES):
                    for ttp in range(NTT):
                        nf = esx * NTT + ttp
                        sc.wait_ge(s_ev1, 192 + nf + 1)
                        if nf >= 2:
                            sc.wait_ge(s_gsl[nf % 2], 16 * (nf // 2))
                        sc.activation(
                            gstage[:, (nf % 2) * ES:(nf % 2) * ES + ES],
                            fstage[:, (nf % 2) * ES:(nf % 2) * ES + ES],
                            AF.Silu, bias=C_ZERO, scale=C_ONE
                            ).then_inc(s_silu, 1)

            @blkA.vector
            def _(v):
                for tt in range(NTT):
                    v.wait_ge(s_x0sl[tt % 2], 16 * (tt // 2 + 1))
                    if tt > 0:
                        v.wait_ge(s_ln, 4 * tt)
                    xt = x_sb[:, (tt % 2) * D:(tt % 2) * D + D]
                    v.reduce_sum(st[:, 0:1], xt, axis=AX.X).then_inc(s_ln, 1)
                    v.wait_ge(s_ln, 4 * tt + 1)
                    v.tensor_scalar_mul(st[:, 1:2], st[:, 0:1], 1.0 / D).then_inc(s_ln, 1)
                    v.wait_ge(s_ln, 4 * tt + 2)
                    if tt > 0:
                        v.wait_ge(s_xnst, 16 * tt)
                    v.scalar_tensor_tensor(
                        xn_sb[:, :], xt, st[:, 1:2], xt,
                        ALU.subtract, ALU.mult, accum_out=st[:, 2:3],
                    ).then_inc(s_ln, 1)
                    v.wait_ge(s_act0, 2 * (tt + 1))
                    v.tensor_scalar(
                        xn_sb[:, :], xt, st[:, 1:2], st[:, 4:5],
                        ALU.subtract, ALU.mult,
                    ).then_inc(s_ln, 1)
                n1 = 0
                for wi, (wname, dest) in enumerate(W_ORDER[:3]):
                    for esx in range(NES):
                        idx = wi * NES + esx
                        for tt in range(NTT):
                            v.wait_ge(s_mm1, idx * NTT + tt + 1)
                            if n1 >= 4:
                                v.wait_ge(s_stgsl[n1 % 4], 16 * (n1 // 4))
                            v.tensor_copy(
                                stage[:, (n1 % 4) * ES:(n1 % 4) * ES + ES],
                                psA[tt][:, :],
                            ).then_inc(s_ev1, 1)
                            n1 += 1
                for ch in range(8):
                    v.wait_ge(s_mm1, 192 + ch + 1)
                    v.tensor_copy(biasb[:, ch * ES:ch * ES + ES], psA[ch][:, :]
                                  ).then_inc(s_biasb, 1)
                nf = 0
                for esx in range(NES):
                    for tt in range(NTT):
                        v.wait_ge(s_mm1, 200 + esx * NTT + tt + 1)
                        if nf == 0:
                            v.wait_ge(s_biasb, 8)
                        if nf >= 2:
                            v.wait_ge(s_silu, nf - 1)
                        v.tensor_add(
                            fstage[:, (nf % 2) * ES:(nf % 2) * ES + ES],
                            psA[tt][:, :],
                            biasb[:, esx * ES:esx * ES + ES],
                        ).then_inc(s_ev1, 1)
                        nf += 1

            @blkA.tensor
            def _(t):
                t.wait_ge(s_c, 48)
                t.wait_ge(s_xnT, 16 * NDC)

                def proj(idx, first_bank_wait):
                    for dcx in range(NDC):
                        G = idx * NDC + dcx
                        slot = G % WSLOTS
                        t.wait_ge(s_wtsl[slot], 16 * (G // WSLOTS + 1))
                        for tt in range(NTT):
                            if dcx == 0:
                                first_bank_wait(tt)
                            mm = t.matmul(
                                psA[tt][:, :],
                                xnT[:, dcx * T + tt * 128:dcx * T + tt * 128 + 128],
                                wt_sb[:, slot * ES:slot * ES + ES],
                                start=(dcx == 0), stop=(dcx == NDC - 1))
                            if dcx == NDC - 1:
                                mm.then_inc(s_mm1, 1)
                            elif tt == NTT - 1:
                                mm.then_inc(s_slabrel, 1)

                for wi, (wname, dest) in enumerate(W_ORDER[:3]):
                    for esx in range(NES):
                        idx = wi * NES + esx
                        if idx == 0:
                            proj(idx, lambda tt: None)
                        else:
                            proj(idx, lambda tt, i=idx: t.wait_ge(
                                s_ev1, (i - 1) * NTT + tt + 1))
                for ch in range(8):
                    t.wait_ge(s_browsl[ch % 2], 16 * (ch // 2 + 1))
                    t.wait_ge(s_ev1, 184 + ch + 1)
                    t.matmul(psA[ch][:, :], ones_f[:, :],
                             brow[:, (ch % 2) * ES:(ch % 2) * ES + ES],
                             start=True, stop=True).then_inc(s_mm1, 1)
                for esx in range(NES):
                    idx = 24 + esx
                    if esx == 0:
                        proj(idx, lambda tt: t.wait_ge(s_biasb, tt + 1))
                    else:
                        proj(idx, lambda tt, e=esx: t.wait_ge(
                            s_ev1, 192 + (e - 1) * NTT + tt + 1))

        if phases < 2:
            if debug:
                with nc.Block("dump1") as blkD:
                    @blkD.gpsimd
                    def _(g):
                        g.wait_ge(s_gsl[0], 16 * 32)
                        g.wait_ge(s_gsl[1], 16 * 32)
                        for w in ("k", "v", "q", "f", "o"):
                            g.wait_ge(s_ag[w], 1)
                        g.wait_ge(s_ccK, 1)
                        g.wait_ge(s_ccV, 1)
                        nd = 0
                        for name, srcb in (("xn", xn_dram), ("q", q_dram),
                                           ("g", g_dram), ("wq", w_bf["q"]),
                                           ("wf", w_bf["f"]), ("kg", kg), ("vg", vg)):
                            g.dma_start(out=dbg[name][:, :], in_=srcb[:, :]
                                        ).then_inc(s_o2ld, 16)
                            nd += 1
                        g.wait_ge(s_o2ld, 16 * nd)
            return nc
        es_b = ExitStack()
        with es_b:
            sbufB = lambda n, s, d: es_b.enter_context(nc.sbuf_tensor(n, s, d))
            NB = NH * NKB
            KSL = NKB * KBS
            VSL = NKB * 4 * 256
            qt_sb = sbufB("qt_sb", [128, 3 * T], BF16)
            kt_sb = sbufB("kt_sb", [128, 2 * KSL], BF16)
            v_sb = sbufB("v_sb", [128, 2 * VSL], BF16)
            e_sb = sbufB("e_sb", [128, 8 * T], BF16)
            eA_sb = sbufB("eA_sb", [128, 2 * T], BF16)
            eB_sb = sbufB("eB_sb", [128, 2 * T], BF16)
            esum_sb = sbufB("esum_sb", [128, 2 * T], BF16)
            av_buf = sbufB("av_buf", [128, 2 * T], F32)
            z_buf = sbufB("z_buf", [128, 2 * T], F32)
            zinv = sbufB("zinv", [128, T], F32)
            o2st_sb = sbufB("o2st_sb", [128, 2 * T], BF16)
            ps_sc = [es_b.enter_context(nc.psum_tensor(f"ps_sc{i}", [128, T], F32))
                     for i in range(2)]
            ps_av = es_b.enter_context(nc.psum_tensor("ps_av", [128, T], F32))
            ps_zb = es_b.enter_context(nc.psum_tensor("ps_zb", [128, T], F32))

            blkB = es_b.enter_context(nc.Block("ph2"))

            def ph2_entry(eng):
                for j in range(2):
                    eng.wait_ge(s_gsl[j], 16 * 32)
                eng.wait_ge(s_silu, 64)
                eng.wait_ge(s_ev1, 256)
                eng.wait_ge(s_mm1, 264)

            @blkB.sync
            def _(s):
                s.wait_ge(s_ccK, 1)
                ph2_entry(s)
                for j in range(4):
                    s.wait_ge(s_stgsl[j], 16 * 48)

                def emit_qt(h):
                    s.dma_start(out=qt_sb[:, (h % 3) * T:(h % 3) * T + T],
                                in_=q_dram[0:T, h * 128:h * 128 + 128],
                                transpose=True).then_inc(s_qt, 16)

                def emit_kt(h, kb):
                    kof = (h % 2) * KSL + kb * KBS
                    s.dma_start(out=kt_sb[:, kof:kof + KBS],
                                in_=kg[kb * KBS:kb * KBS + KBS,
                                       h * 128:h * 128 + 128],
                                transpose=True).then_inc(s_kt, 16)

                for h in range(3):
                    emit_qt(h)
                for h in range(2):
                    for kb in range(NKB):
                        emit_kt(h, kb)
                for h in range(NH):
                    if h + 2 < NH:
                        for kb in range(NKB):
                            s.wait_ge(s_avz, 4 * (NKB * h + kb + 1))
                            emit_kt(h + 2, kb)
                    if h + 3 < NH:
                        s.wait_ge(s_avz, 4 * NKB * (h + 1))
                        emit_qt(h + 3)
                    if h >= 1:
                        s.wait_ge(s_o2, 2 * h)
                        s.dma_start(
                            out=o2_dram[(h - 1) * 128:(h - 1) * 128 + 128, :],
                            in_=o2st_sb[:, ((h - 1) % 2) * T:((h - 1) % 2) * T + T],
                        ).then_inc(s_o2sl[(h - 1) % 2], 16)
                s.wait_ge(s_o2, 64)
                s.dma_start(
                    out=o2_dram[31 * 128:31 * 128 + 128, :],
                    in_=o2st_sb[:, T:2 * T],
                ).then_inc(s_o2sl[1], 16)

            @blkB.scalar
            def _(sc):
                ph2_entry(sc)
                for gidx in range(4 * NB):
                    sc.wait_ge(s_sc, gidx + 1)
                    if gidx >= 8:
                        sc.wait_ge(s_avz, gidx - 7)
                        bp, ksp = divmod(gidx - 8, 4)
                        if ksp < 2:
                            sc.wait_ge(s_esA, bp + 1)
                        else:
                            sc.wait_ge(s_dve_es, 2 * bp + 1)
                    sl = (gidx % 8) * T
                    sc.activation(
                        e_sb[:, sl:sl + T],
                        ps_sc[gidx % 2][:, :],
                        AF.Exp, bias=C_EXPB, scale=C_EXPS,
                    ).then_inc(s_exp, 1)

            @blkB.tensor
            def _(t):
                t.wait_ge(s_ev1, 256)

                def sc_part(b):
                    h, kb = divmod(b, NKB)
                    if kb == 0:
                        t.wait_ge(s_qt, 16 * (h + 1))
                    t.wait_ge(s_kt, 16 * (b + 1))
                    kof = (h % 2) * KSL + kb * KBS
                    qof = (h % 3) * T
                    for ks in range(4):
                        gidx = b * 4 + ks
                        if gidx >= 2:
                            t.wait_ge(s_exp, gidx - 1)
                        t.matmul(ps_sc[gidx % 2][:, 0:512],
                                 kt_sb[:, kof + ks * 128:kof + ks * 128 + 128],
                                 qt_sb[:, qof:qof + 512],
                                 start=True, stop=True)
                        t.matmul(ps_sc[gidx % 2][:, 512:1024],
                                 kt_sb[:, kof + ks * 128:kof + ks * 128 + 128],
                                 qt_sb[:, qof + 512:qof + 1024],
                                 start=True, stop=True).then_inc(s_sc, 1)

                def avz_part(b):
                    h, kb = divmod(b, NKB)
                    p, hh = divmod(h, 2)
                    t.wait_ge(s_v, 16 * (NKB * p + kb + 1))
                    if kb == 0 and h >= 1:
                        t.wait_ge(s_evh, 2 * h)
                    st0 = (kb == 0)
                    for ks in range(4):
                        gidx = b * 4 + ks
                        t.wait_ge(s_exp, gidx + 1)
                        sl = (gidx % 8) * T
                        vof = (p % 2) * VSL + kb * 1024 + ks * 256 + hh * 128
                        vtile = v_sb[:, vof:vof + 128]
                        t.matmul(ps_av[:, 0:512], vtile,
                                 e_sb[:, sl:sl + 512],
                                 start=(st0 and ks == 0),
                                 stop=(kb == NKB - 1 and ks == 3))
                        t.matmul(ps_av[:, 512:1024], vtile,
                                 e_sb[:, sl + 512:sl + 1024],
                                 start=(st0 and ks == 0),
                                 stop=(kb == NKB - 1 and ks == 3)
                                 ).then_inc(s_avz, 1)
                    t.wait_ge(s_dve_es, 2 * (b + 1))
                    eso = (b % 2) * T
                    t.matmul(ps_zb[:, 0:512], ones_bf[:, :],
                             esum_sb[:, eso:eso + 512],
                             start=st0, stop=(kb == NKB - 1))
                    t.matmul(ps_zb[:, 512:1024], ones_bf[:, :],
                             esum_sb[:, eso + 512:eso + 1024],
                             start=st0, stop=(kb == NKB - 1)
                             ).then_inc(s_zmm, 1)

                sc_part(0)
                for b in range(NB):
                    if b + 1 < NB:
                        sc_part(b + 1)
                    avz_part(b)

            @blkB.vector
            def _(v):
                ph2_entry(v)
                for b in range(NB):
                    h, kb = divmod(b, NKB)
                    v.wait_ge(s_exp, 4 * b + 4)
                    if b >= 2:
                        v.wait_ge(s_zmm, b - 1)
                    eo = (b % 2) * T
                    s2 = ((4 * b + 2) % 8) * T
                    s3 = ((4 * b + 3) % 8) * T
                    v.tensor_add(eB_sb[:, eo:eo + T],
                                 e_sb[:, s2:s2 + T], e_sb[:, s3:s3 + T]
                                 ).then_inc(s_dve_es, 1)
                    v.wait_ge(s_dve_es, 2 * b + 1)
                    v.wait_ge(s_esA, b + 1)
                    v.tensor_add(esum_sb[:, eo:eo + T],
                                 eA_sb[:, eo:eo + T], eB_sb[:, eo:eo + T]
                                 ).then_inc(s_dve_es, 1)
                    if kb == NKB - 1:
                        ho = (h % 2) * T
                        if h >= 2:
                            v.wait_ge(s_o2, 2 * (h - 1))
                        v.wait_ge(s_avz, 4 * NKB * (h + 1))
                        v.tensor_copy(av_buf[:, ho:ho + T], ps_av[:, :]
                                      ).then_inc(s_evh, 1)
                        v.wait_ge(s_zmm, NKB * (h + 1))
                        v.tensor_copy(z_buf[:, ho:ho + T], ps_zb[:, :]
                                      ).then_inc(s_evh, 1)
                        v.wait_ge(s_evh, 2 * h + 2)
                        v.wait_ge(s_o2, 2 * h)
                        v.reciprocal(zinv[:, :], z_buf[:, ho:ho + T]
                                     ).then_inc(s_o2, 1)
                        v.wait_ge(s_o2, 2 * h + 1)
                        if h >= 2:
                            v.wait_ge(s_o2sl[h % 2], 16 * (h // 2))
                        v.tensor_mul(o2st_sb[:, ho:ho + T],
                                     av_buf[:, ho:ho + T], zinv[:, :]
                                     ).then_inc(s_o2, 1)

            @blkB.gpsimd
            def _(g):
                g.wait_ge(s_ccV, 1)
                ph2_entry(g)

                def emit_v(p, kb):
                    vof = (p % 2) * VSL + kb * 1024
                    g.dma_start(
                        out=v_sb[:, vof:vof + 1024].rearrange(
                            "p (ks c) -> p ks c", ks=4),
                        in_=vg[kb * KBS:kb * KBS + KBS,
                               p * 256:p * 256 + 256].rearrange(
                            "(ks k) c -> k ks c", k=128),
                    ).then_inc(s_v, 16)

                for p in range(2):
                    for kb in range(NKB):
                        emit_v(p, kb)
                for b in range(NB):
                    h, kb = divmod(b, NKB)
                    if h % 2 == 0 and h >= 2 and h + 2 < NH:
                        g.wait_ge(s_avz, 4 * (NKB * (h - 1) + kb + 1))
                        emit_v(h // 2 + 1, kb)
                    g.wait_ge(s_exp, 4 * b + 2)
                    if b >= 2:
                        g.wait_ge(s_zmm, b - 1)
                    eo = (b % 2) * T
                    s0 = ((4 * b) % 8) * T
                    s1 = ((4 * b + 1) % 8) * T
                    g.tensor_tensor(
                        eA_sb[:, eo:eo + T],
                        e_sb[:, s0:s0 + T], e_sb[:, s1:s1 + T],
                        ALU.add).then_inc(s_esA, 1)

        if phases < 3:
            if debug:
                with nc.Block("dump2") as blkD:
                    @blkD.gpsimd
                    def _(g):
                        g.wait_ge(s_o2sl[0], 16 * 16)
                        g.wait_ge(s_o2sl[1], 16 * 16)
                        g.wait_ge(s_gsl[0], 16 * 32)
                        g.wait_ge(s_gsl[1], 16 * 32)
                        nd = 0
                        for name, srcb in (("xn", xn_dram), ("q", q_dram),
                                           ("g", g_dram), ("o2", o2_dram),
                                           ("wq", w_bf["q"]), ("wf", w_bf["f"]),
                                           ("kg", kg), ("vg", vg)):
                            g.dma_start(out=dbg[name][:, :], in_=srcb[:, :]
                                        ).then_inc(s_o2ld, 16)
                            nd += 1
                        g.wait_ge(s_o2ld, 16 * nd)
            return nc
        es_c = ExitStack()
        with es_c:
            sbufC = lambda n, s, d: es_c.enter_context(nc.sbuf_tensor(n, s, d))
            o2_sb = sbufC("o2_sb", [128, NDC * T], BF16)
            wot_sb = sbufC("wot_sb", [128, WSLOTS * ES], BF16)
            x3_sb = sbufC("x3_sb", [128, 4 * ES], F32)
            g3_sb = sbufC("g3_sb", [128, 4 * ES], F32)
            res_sb = sbufC("res_sb", [128, 4 * ES], F32)
            psB = [es_c.enter_context(nc.psum_tensor(f"psB{i}", [128, ES], F32))
                   for i in range(8)]

            blkC = es_c.enter_context(nc.Block("ph3"))

            def emit_wot(eng, os_, e):
                G = os_ * NDC + e
                if G >= WSLOTS:
                    idxp, dp = divmod(G - WSLOTS, NDC)
                    if dp == NDC - 1:
                        eng.wait_ge(s_mm3, idxp * NTT + NTT)
                    else:
                        eng.wait_ge(s_wotrel, idxp * (NDC - 1) + dp + 1)
                slot = G % WSLOTS
                eng.dma_start(
                    out=wot_sb[:, slot * ES:slot * ES + ES],
                    in_=w_bf["o"][e * 128:e * 128 + 128, os_ * ES:os_ * ES + ES],
                ).then_inc(s_wotsl[slot], 16)

            def ph3_barrier(eng):
                eng.wait_ge(s_avz, 2048)
                eng.wait_ge(s_o2, 64)
                eng.wait_ge(s_o2sl[0], 16 * 16)
                eng.wait_ge(s_o2sl[1], 16 * 16)
                eng.wait_ge(s_ag["o"], 1)

            @blkC.sync
            def _(s):
                ph3_barrier(s)
                for os_ in range(NES):
                    for tt in range(NTT):
                        n = os_ * NTT + tt
                        if n >= 4:
                            s.wait_ge(s_ev3, 2 * (n - 3))
                        s.dma_start(
                            out=x3_sb[:, (n % 4) * ES:(n % 4) * ES + ES],
                            in_=x_in[tt * 128:tt * 128 + 128, os_ * ES:os_ * ES + ES],
                        ).then_inc(s_x3sl[n % 4], 16)

            @blkC.scalar
            def _(sc):
                ph3_barrier(sc)
                for os_ in range(NES):
                    for e in range(NDC):
                        emit_wot(sc, os_, e)

            @blkC.gpsimd
            def _(g):
                g.wait_ge(s_o2sl[0], 16 * 16)
                g.wait_ge(s_o2sl[1], 16 * 16)
                for e in range(NDC):
                    g.dma_start(out=o2_sb[:, e * T:e * T + T],
                                in_=o2_dram[e * 128:e * 128 + 128, :]
                                ).then_inc(s_o2ld, 16)
                g.wait_ge(s_gsl[0], 16 * 32)
                g.wait_ge(s_gsl[1], 16 * 32)
                for n in range(68):
                    if n < 64:
                        os_, tt = n // NTT, n % NTT
                        if n >= 4:
                            g.wait_ge(s_ev3, 2 * (n - 3))
                        g.dma_start(
                            out=g3_sb[:, (n % 4) * ES:(n % 4) * ES + ES],
                            in_=g_dram[tt * 128:tt * 128 + 128,
                                       os_ * ES:os_ * ES + ES],
                        ).then_inc(s_g3sl[n % 4], 16)
                    if n >= 4:
                        ns = n - 4
                        os_, tt = ns // NTT, ns % NTT
                        g.wait_ge(s_ev3, 2 * (ns + 1))
                        g.dma_start(
                            out=out_t[tt * 128:tt * 128 + 128,
                                      os_ * ES:os_ * ES + ES],
                            in_=res_sb[:, (ns % 4) * ES:(ns % 4) * ES + ES],
                        ).then_inc(s_outsl[ns % 4], 16)
                for j in range(4):
                    g.wait_ge(s_outsl[j], 16 * 16)
                if debug:
                    g.dma_start(out=dbg["xn"][:, :], in_=xn_dram[:, :]).then_inc(s_o2ld, 16)
                    g.dma_start(out=dbg["q"][:, :], in_=q_dram[:, :]).then_inc(s_o2ld, 16)
                    g.dma_start(out=dbg["g"][:, :], in_=g_dram[:, :]).then_inc(s_o2ld, 16)
                    g.dma_start(out=dbg["o2"][:, :], in_=o2_dram[:, :]).then_inc(s_o2ld, 16)
                    g.dma_start(out=dbg["wq"][:, :], in_=w_bf["q"][:, :]).then_inc(s_o2ld, 16)
                    g.dma_start(out=dbg["wf"][:, :], in_=w_bf["f"][:, :]).then_inc(s_o2ld, 16)
                    g.dma_start(out=dbg["kg"][:, :], in_=kg[:, :]).then_inc(s_o2ld, 16)
                    g.dma_start(out=dbg["vg"][:, :], in_=vg[:, :]).then_inc(s_o2ld, 16)
                    g.wait_ge(s_o2ld, 16 * (NDC + 8))

            @blkC.tensor
            def _(t):
                t.wait_ge(s_o2, 64)
                t.wait_ge(s_exp, 2048)
                t.wait_ge(s_zmm, 512)
                t.wait_ge(s_o2ld, 16 * NDC)
                for os_ in range(NES):
                    for e in range(NDC):
                        G = os_ * NDC + e
                        slot = G % WSLOTS
                        t.wait_ge(s_wotsl[slot],
                                  16 * (WT_BASE[slot] + G // WSLOTS + 1))
                        for tt in range(NTT):
                            if e == 0 and os_ >= 1:
                                t.wait_ge(s_ev3, 2 * ((os_ - 1) * NTT + tt + 1))
                            mm = t.matmul(
                                psB[tt][:, :],
                                o2_sb[:, e * T + tt * 128:e * T + tt * 128 + 128],
                                wot_sb[:, slot * ES:slot * ES + ES],
                                start=(e == 0), stop=(e == NDC - 1))
                            if e == NDC - 1:
                                mm.then_inc(s_mm3, 1)
                            elif tt == NTT - 1:
                                mm.then_inc(s_wotrel, 1)

            @blkC.vector
            def _(v):
                for os_ in range(NES):
                    for tt in range(NTT):
                        n = os_ * NTT + tt
                        v.wait_ge(s_mm3, n + 1)
                        v.wait_ge(s_g3sl[n % 4], 16 * (n // 4 + 1))
                        v.wait_ge(s_x3sl[n % 4], 16 * (n // 4 + 1))
                        if n >= 4:
                            v.wait_ge(s_outsl[n % 4], 16 * (n // 4))
                        rof = (n % 4) * ES
                        v.tensor_mul(res_sb[:, rof:rof + ES], psB[tt][:, :],
                                     g3_sb[:, rof:rof + ES]).then_inc(s_ev3, 1)
                        v.wait_ge(s_ev3, 2 * n + 1)
                        v.tensor_add(res_sb[:, rof:rof + ES],
                                     res_sb[:, rof:rof + ES],
                                     x3_sb[:, rof:rof + ES]).then_inc(s_ev3, 1)

    return nc


_NC_CACHE = None


def _get_nc():
    global _NC_CACHE
    if _NC_CACHE is None:
        _NC_CACHE = build_program()
    return _NC_CACHE


def kernel(L0_attn_x, L0_attn_w_q, L0_attn_w_k, L0_attn_w_v, L0_attn_w_o,
           L0_ffn_weight, L0_ffn_bias):
    x = np.ascontiguousarray(np.asarray(L0_attn_x, dtype=np.float32).reshape(S, D))
    ws = {
        "q": np.asarray(L0_attn_w_q, dtype=np.float32).T,
        "k": np.asarray(L0_attn_w_k, dtype=np.float32).T,
        "v": np.asarray(L0_attn_w_v, dtype=np.float32).T,
        "o": np.asarray(L0_attn_w_o, dtype=np.float32).T,
        "f": np.asarray(L0_ffn_weight, dtype=np.float32).T,
    }
    bias = np.ascontiguousarray(
        np.asarray(L0_ffn_bias, dtype=np.float32).reshape(1, D))

    consts = np.zeros((128, 8), dtype=np.float32)
    consts[:, 0] = EXP_SCALE
    consts[:, 1] = EXP_BIAS
    consts[:, 2] = 1.0 / 4096.0
    consts[:, 3] = EPS
    consts[:, 4] = -0.5
    consts[:, 5] = 1.0
    consts[:, 6] = 0.0
    ones_bf = np.ones((128, 128), dtype=ml_dtypes.bfloat16)
    ones_f = np.ones((1, 128), dtype=np.float32)

    nc = _get_nc()
    ones_f128 = np.ones((128, 128), dtype=np.float32)
    in_maps = []
    for c in range(8):
        m = {
            "x": x[c * T:(c + 1) * T, :],
            "bias": bias, "consts": consts,
            "ones_bf": ones_bf, "ones_f": ones_f, "ones_f128": ones_f128,
        }
        for w in ("q", "k", "v", "o", "f"):
            m[f"w{w}_sh"] = np.ascontiguousarray(ws[w][c * WSH:(c + 1) * WSH, :])
        in_maps.append(m)
    res = run_bass_kernel_spmd(nc, in_maps, core_ids=list(range(8)))
    out = np.concatenate([np.asarray(r["out"]) for r in res.results], axis=0)
    return out.reshape(1, S, D).astype(np.float32)
